# version 27
# baseline (speedup 1.0000x reference)
import contextlib
import numpy as np

import concourse.bass as bass
import concourse.mybir as mybir
from concourse.bass_utils import run_bass_kernel_spmd

F32 = mybir.dt.float32
I32 = mybir.dt.int32
Alu = mybir.AluOpType
Act = mybir.ActivationFunctionType
AxX = mybir.AxisListType.X

NCORES = 8
SEQ = 2
S = 2048
E = 1024
EC = E // 128
H = 16
FF = 2816
FFC = FF // 128
DIN = 2048
DINC = DIN // 128
NT = 512
TPS = S // NT
KC = S // 128

ENGINES = ("T", "V", "S", "P", "Y")
DQ_OF = {"Y": "dY", "P": "dP", "S": "dS"}
NLANES = 10


class Prog:

    def __init__(self):
        self.items = {e: [] for e in ENGINES}
        lanes = [f"{dq}{i}" for dq in DQ_OF.values() for i in range(NLANES)]
        self.lanes = lanes
        self.tick = {k: 0 for k in list(ENGINES) + lanes}
        self.ndma = {dq: 0 for dq in DQ_OF.values()}
        self.know = {e: {} for e in ENGINES}
        self.vc = {k: {} for k in list(ENGINES) + lanes}
        self.writes = {}
        self.reads = {}

    @staticmethod
    def _span(ap):
        key = ap.tensor.name
        lo = int(ap.offset)
        span = 0
        for st, ct in ap.ap:
            span += (int(ct) - 1) * abs(int(st))
        return key, lo, lo + span + 1

    def _collect(self, aps_r, aps_w):
        deps = set()
        for ap in aps_r:
            key, lo, hi = self._span(ap)
            for (l2, h2, cid) in self.writes.get(key, ()):
                if l2 < hi and lo < h2:
                    deps.add(cid)
        for ap in aps_w:
            key, lo, hi = self._span(ap)
            for (l2, h2, cid) in self.writes.get(key, ()):
                if l2 < hi and lo < h2:
                    deps.add(cid)
            for (l2, h2, cid) in self.reads.get(key, ()):
                if l2 < hi and lo < h2:
                    deps.add(cid)
        return deps

    def _record(self, aps_r, aps_w, cid):
        for ap in aps_r:
            key, lo, hi = self._span(ap)
            rl = self.reads.setdefault(key, [])
            rl.append((lo, hi, cid))
            if len(rl) > 256:
                del rl[:128]
        for ap in aps_w:
            key, lo, hi = self._span(ap)
            wl = self.writes.setdefault(key, [])
            wl.append((lo, hi, cid))
            if len(wl) > 192:
                del wl[:96]

    def _emit_wait(self, e, cid):
        pe, t = cid
        if self.know[e].get(pe, 0) >= t:
            return
        self.items[e].append(("w", pe, t))
        self.know[e][pe] = t
        snap = self.vc[pe].get(t)
        if snap:
            k = self.know[e]
            for sk, sv in snap.items():
                if k.get(sk, 0) < sv:
                    k[sk] = sv

    def op(self, e, emit, reads=(), writes=()):
        deps = self._collect(reads, writes)
        for cid in sorted(deps, key=str):
            if e == "T" and cid[0] == "T":
                continue
            self._emit_wait(e, cid)
        self.tick[e] += 1
        t = self.tick[e]
        self.items[e].append(("o", emit))
        snap = dict(self.know[e])
        snap[e] = t
        self.vc[e][t] = snap
        self._record(reads, writes, (e, t))

    def raw(self, e, emit):
        self.items[e].append(("r", emit))

    def dma(self, qe, emit, reads=(), writes=()):
        dq = DQ_OF[qe]
        lane = f"{dq}{self.ndma[dq] % NLANES}"
        self.ndma[dq] += 1
        deps = self._collect(reads, writes)
        if self.tick[lane] > 0:
            deps.add((lane, self.tick[lane]))
        for cid in sorted(deps, key=str):
            self._emit_wait(qe, cid)
        self.tick[lane] += 16
        t = self.tick[lane]
        self.items[qe].append(("d", emit, lane))
        snap = dict(self.know[qe])
        snap[lane] = t
        self.vc[lane][t] = snap
        self._record(reads, writes, (lane, t))

    def finalize(self):
        for qe, dq in DQ_OF.items():
            for i in range(NLANES):
                lane = f"{dq}{i}"
                if self.tick[lane] > 0:
                    self.items[qe].append(("w", lane, self.tick[lane]))

    def emit(self, block, sems):
        secmap = {"T": block.tensor, "V": block.vector, "S": block.scalar,
                  "P": block.gpsimd, "Y": block.sync}
        for e in ENGINES:
            items = self.items[e]

            def emit_items(eng, items=items, e=e):
                for it in items:
                    if it[0] == "w":
                        eng.wait_ge(sems[it[1]], it[2])
                    elif it[0] == "o":
                        it[1](eng).then_inc(sems[e], 1)
                    elif it[0] == "r":
                        it[1](eng)
                    else:
                        it[1](eng).then_inc(sems[it[2]], 16)

            secmap[e](emit_items)


def build_nc(stop_after=None):
    nc = bass.Bass("TRN2", target_bir_lowering=False)
    p = Prog()

    dt = lambda name, shape, kind: nc.dram_tensor(name, shape, F32, kind=kind).ap()

    x_in = dt("x_in", [SEQ, EC, 128, S], "ExternalInput")
    w_qkv = dt("w_qkv", [24, 128, EC * 128], "ExternalInput")
    b_qkv = dt("b_qkv", [128, 24], "ExternalInput")
    w_o = dt("w_o", [EC, 128, EC * 128], "ExternalInput")
    b_o = dt("b_o", [128, EC], "ExternalInput")
    w_i = dt("w_i", [DINC, 128, EC * 128], "ExternalInput")
    b_i = dt("b_i", [128, DINC], "ExternalInput")
    w_g = dt("w_g", [DINC, 128, DINC * 128], "ExternalInput")
    b_g = dt("b_g", [128, DINC], "ExternalInput")
    w_o2 = dt("w_o2", [EC, 128, DINC * 128], "ExternalInput")
    b_o2 = dt("b_o2", [128, EC], "ExternalInput")
    nw_i = dt("nw_i", [128, EC], "ExternalInput")
    nb_i = dt("nb_i", [128, EC], "ExternalInput")
    bsp = dt("bsp", [128, DINC], "ExternalInput")
    ddv = dt("ddv", [1, 1], "ExternalInput")
    wg_raw = dt("wg_raw", [FFC, 128, E], "ExternalInput")
    wu_raw = dt("wu_raw", [FFC, 128, E], "ExternalInput")
    wd_raw = dt("wd_raw", [EC, 128, FF], "ExternalInput")
    y_out = dt("y_out", [SEQ, EC, 128, S], "ExternalOutput")

    wq_gu = dt("wq_gu", [2 * FFC, 128, EC * 128], "Internal")
    wq_d = dt("wq_d", [EC, 128, FFC * 128], "Internal")
    k_s = dt("k_s", [EC, 128, S], "Internal")
    v_s = dt("v_s", [KC, 128, H, 64], "Internal")
    xa_s = dt("xa_s", [EC, 128, S], "Internal")
    xb_s = dt("xb_s", [EC, 128, S], "Internal")

    sems = {}
    semkeys = ["T", "V", "S", "P"] + [f"{dq}{i}" for dq in DQ_OF.values()
                                      for i in range(NLANES)]
    for k in semkeys:
        sems[k] = nc.alloc_semaphore(f"sem_{k}")

    ctx = contextlib.ExitStack()
    EE = ctx.enter_context

    def sb(name, cols, dtp=F32):
        return EE(nc.sbuf_tensor(name, [128, cols], dtp))

    xtile = sb("xtile", EC * NT)
    lntile = sb("lntile", EC * NT)
    sqtile = sb("sqtile", EC * NT)
    qtile = sb("qtile", EC * NT)
    pvu = sb("pvu", EC * NT)
    bigA = sb("bigA", FFC * NT)
    bigB = sb("bigB", FFC * NT)
    wtile0 = sb("wtile0", 1024)
    wtile1 = sb("wtile1", 1024)
    stage = sb("stage", 2048)
    accA = sb("accA", NT)
    accB = sb("accB", NT)
    small = sb("small", 1024)
    small2 = sb("small2", 1024)
    srows = sb("srows", NT)
    ones128 = sb("ones128", 128)
    onesK = sb("onesK", 128)
    identp = sb("identp", 128)
    identi = sb("identi", 128, I32)
    ditile = sb("ditile", 2 * NT, I32)
    scl = sb("scl", 256)
    rscl = sb("rscl", 256)
    bias_all = sb("bias_all", 160)
    negb = sb("negb", 16)
    kmax2 = sb("kmax2", 16)
    qmax2 = sb("qmax2", 16)
    bnst = sb("bnst", 16)
    mv = sb("mv", 8)
    dd_sb = sb("dd_sb", 4)
    r6 = sb("r6", 2)
    epsb = sb("epsb", 1)

    ps = [EE(nc.psum_tensor(f"ps{i}", [128, 512], F32)) for i in range(8)]

    pvacc = bigA[:, 0:H * NT]
    dx4 = bigA[:, H * NT:H * NT + 4 * NT]
    ubuf = bigB[:, 0:1024]
    pbuf = bigB[:, 1024:2048]
    ktgo = 2048
    ktg = bigB[:, ktgo:ktgo + EC * NT]
    vtgo = ktgo + EC * NT
    VSL = 65
    vtg = bigB[:, vtgo:vtgo + 4 * H * VSL]

    def TT(out, a, b, op):
        p.op("V", lambda v, out=out, a=a, b=b, op=op:
             v.tensor_tensor(out=out, in0=a, in1=b, op=op),
             reads=[a, b], writes=[out])

    def TS(out, a, s1, op0, s2=None, op1=Alu.bypass):
        rd = [a] + [x for x in (s1, s2) if hasattr(x, "tensor")]
        p.op("V", lambda v, out=out, a=a, s1=s1, s2=s2, op0=op0, op1=op1:
             v.tensor_scalar(out=out, in0=a, scalar1=s1, scalar2=s2, op0=op0, op1=op1),
             reads=rd, writes=[out])

    def STT(out, a, scalar, b, op0, op1):
        rd = [a, b] + ([scalar] if hasattr(scalar, "tensor") else [])
        p.op("V", lambda v, out=out, a=a, sc=scalar, b=b, op0=op0, op1=op1:
             v.scalar_tensor_tensor(out=out, in0=a, scalar=sc, in1=b, op0=op0, op1=op1),
             reads=rd, writes=[out])

    def ACT(out, a, func, bias=None, scale=None):
        kw = {}
        rd = [a]
        if bias is not None:
            kw["bias"] = bias
            if hasattr(bias, "tensor"):
                rd.append(bias)
        if scale is not None:
            kw["scale"] = scale
            if hasattr(scale, "tensor"):
                rd.append(scale)
        p.op("S", lambda s, out=out, a=a, func=func, kw=kw:
             s.activation(out=out, in_=a, func=func, **kw), reads=rd, writes=[out])

    def MM(out, lhsT, rhs, start, stop, tile_position=None):
        p.op("T", lambda t, out=out, l=lhsT, r=rhs, st=start, sp=stop, tp=tile_position:
             t.matmul(out, l, r, start=st, stop=sp, tile_position=tp),
             reads=[lhsT, rhs], writes=[out])

    def TR(out, in_, ident):
        p.op("T", lambda t, out=out, i=in_, d=ident: t.transpose(out, i, d),
             reads=[in_, ident], writes=[out])

    def CP(out, in_, eng="V"):
        if eng == "S":
            p.op("S", lambda s, out=out, i=in_: s.copy(out=out, in_=i),
                 reads=[in_], writes=[out])
        else:
            p.op("V", lambda v, out=out, i=in_: v.tensor_copy(out=out, in_=i),
                 reads=[in_], writes=[out])

    def MEMSET(ap, val, eng="V"):
        p.op(eng, lambda x, ap=ap, val=val: x.memset(ap, val), reads=[], writes=[ap])

    def RED(out, in_, op, absval=None, negate=None):
        p.op("V", lambda v, out=out, i=in_, op=op, av=absval, ng=negate:
             v.tensor_reduce(out=out, in_=i, axis=AxX, op=op,
                             apply_absolute_value=av, negate=ng),
             reads=[in_], writes=[out])

    def RECIP(out, in_):
        p.op("V", lambda v, out=out, i=in_: v.reciprocal(out=out, in_=i),
             reads=[in_], writes=[out])

    def DMA(q, out, in_):
        p.dma(q, lambda g, out=out, in_=in_: g.dma_start(out=out, in_=in_),
              reads=[in_], writes=[out])

    def IOTA(out, pattern, base, cm):
        p.op("P", lambda g, out=out, pat=pattern, b=base, cm=cm:
             g.iota(out, pattern=pat, base=b, channel_multiplier=cm),
             reads=[], writes=[out])

    strip_state = {"buf": 0, "dual": False}

    def load_piece(src_ap, cols):
        b = wtile0 if strip_state["buf"] == 0 else wtile1
        q = "P" if (strip_state["dual"] and strip_state["buf"]) else "Y"
        strip_state["buf"] ^= 1
        DMA(q, b[:, 0:cols], src_ap)
        return b

    def proj(psbank, wsrc, nchunks, rhsbuf, rhs_off):
        done = 0
        while done < nchunks:
            take = min(8, nchunks - done)
            wt = load_piece(wsrc[:, done * 128:(done + take) * 128], take * 128)
            for j in range(take):
                c = done + j
                MM(ps[psbank][:, 0:NT], wt[:, j * 128:(j + 1) * 128],
                   rhsbuf[:, rhs_off + c * NT:rhs_off + (c + 1) * NT],
                   start=(c == 0), stop=(c == nchunks - 1))
            done += take

    pcount = [0]

    def projp(wsrc, nchunks, rhsbuf, rhs_off=0):
        b = pcount[0] % 2
        pcount[0] += 1
        proj(b, wsrc, nchunks, rhsbuf, rhs_off)
        return b

    def projp4(wsrc, nchunks, rhsbuf, banks=(0, 1, 2, 5)):
        b = banks[pcount[0] % len(banks)]
        pcount[0] += 1
        proj(b, wsrc, nchunks, rhsbuf, 0)
        return b

    MEMSET(ones128[:], 1.0 / 1024.0)
    MEMSET(onesK[:], 1.0)
    MEMSET(r6[:, 1:2], 6.0)
    MEMSET(epsb[:, 0:1], 1e-5)
    RECIP(r6[:, 0:1], r6[:, 1:2])
    IOTA(identi[:], [[-1, 128]], 0, 1)
    TS(identp[:], identi[:], 0, Alu.is_equal)
    DMA("Y", dd_sb[0:1, 0:1], ddv[:, :])
    ACT(dd_sb[0:1, 1:2], dd_sb[0:1, 0:1], Act.Abs)
    TS(dd_sb[0:1, 3:4], dd_sb[0:1, 1:2], -1.0, Alu.mult)
    MM(ps[6][:, 0:1], onesK[0:1, 0:128], dd_sb[0:1, 3:4], start=True, stop=True)
    CP(dd_sb[:, 2:3], ps[6][:, 0:1])
    neg_a = dd_sb[:, 2:3]

    BIAS = {}
    off = 0
    for name, src, n in [("qkv", b_qkv, 24), ("o", b_o, EC), ("i", b_i, DINC),
                         ("g", b_g, DINC), ("o2", b_o2, EC), ("nw", nw_i, EC),
                         ("nb", nb_i, EC), ("bsp", bsp, DINC)]:
        BIAS[name] = off
        DMA("Y", bias_all[:, off:off + n], src[:, :])
        off += n

    def bcol(name, j):
        o = BIAS[name] + j
        return bias_all[:, o:o + 1]

    qscnt = [0]

    def quant_slice(src, dst, c0, cw, b5=5, b6=6, b7=7, acc_off=0, vabs=False):
        so = (qscnt[0] % 8) * 32
        qscnt[0] += 1
        bw = cw // 16
        sview = src[:, c0:c0 + cw].rearrange("p (b k) -> p b k", k=16)
        RED(scl[:, so:so + bw], sview, Alu.max, absval=True)
        TS(scl[:, so:so + bw], scl[:, so:so + bw], r6[:, 0:1], Alu.mult, 1e-12, Alu.max)
        RECIP(rscl[:, so:so + bw], scl[:, so:so + bw])
        rb = rscl[:, so:so + bw].unsqueeze(2).broadcast_to([128, bw, 16])
        sclb = scl[:, so:so + bw].unsqueeze(2).broadcast_to([128, bw, 16])
        xn = ps[b5][:, 0:cw]
        TT(xn.rearrange("p (b k) -> p b k", k=16), sview, rb, Alu.mult)
        sgn = ps[b7][:, 0:cw]
        ACT(sgn, xn, Act.Sign)
        axn = ps[b6][:, 0:cw]
        aA = accA[:, acc_off:acc_off + cw]
        aB = accB[:, acc_off:acc_off + cw]
        if vabs:
            TS(aB, xn, -1.0, Alu.mult)
            TT(axn, xn, aB, Alu.max)
        else:
            ACT(axn, xn, Act.Abs)
        TS(aA, axn, 0.25, Alu.is_gt)
        for thr in (0.75, 1.25, 1.75):
            STT(aA, axn, thr, aA, Alu.is_gt, Alu.add)
        TS(aB, axn, 5.0, Alu.is_gt)
        for thr in (5.0, 3.5, 2.5):
            STT(aB, axn, thr, aB, Alu.is_gt, Alu.add)
        G = ps[b5][:, 0:cw]
        STT(G, aA, 0.5, aB, Alu.mult, Alu.add)
        TT(aA.rearrange("p (b k) -> p b k", k=16),
           sgn.rearrange("p (b k) -> p b k", k=16), sclb, Alu.mult)
        TT(dst[:, c0:c0 + cw], G, aA, Alu.mult)

    def quant(src, dst, ncols):
        for c0 in range(0, ncols, 512):
            quant_slice(src, dst, c0, min(512, ncols - c0))

    def ln_fmajor(src, dst):
        for c in range(EC):
            MM(ps[4][:, 0:NT], ones128[:], src[:, c * NT:(c + 1) * NT],
               start=(c == 0), stop=(c == EC - 1))
        for c in range(EC):
            sq = stage[:, (c % 2) * NT:((c % 2) + 1) * NT]
            ACT(sq, src[:, c * NT:(c + 1) * NT], Act.Square)
            MM(ps[3][:, 0:NT], ones128[:], sq, start=(c == 0), stop=(c == EC - 1))
        CP(small[:, 0:NT], ps[4][:, 0:NT])
        TT(small[:, 512:512 + NT], ps[4][:, 0:NT], small[:, 0:NT], Alu.mult)
        TT(small2[:, 0:NT], ps[3][:, 0:NT], small[:, 512:512 + NT], Alu.subtract)
        ACT(small2[:, 512:512 + NT], small2[:, 0:NT], Act.Sqrt, bias=epsb[:, 0:1])
        RECIP(ps[3][:, 0:NT], small2[:, 512:512 + NT])
        for c in range(EC):
            TT(dst[:, c * NT:(c + 1) * NT], src[:, c * NT:(c + 1) * NT],
               ps[4][:, 0:NT], Alu.subtract)
            TT(dst[:, c * NT:(c + 1) * NT], dst[:, c * NT:(c + 1) * NT],
               ps[3][:, 0:NT], Alu.mult)

    pre_jobs = [("g", wg_raw, s0, E) for s0 in range(FFC)] + \
               [("u", wu_raw, s0, E) for s0 in range(FFC)] + \
               [("d", wd_raw, s0, FF) for s0 in range(EC)]

    def prephase_strip(ji):
        wsel, raw, s0, fi = pre_jobs[ji]
        par = (ji % 2) * FF
        DMA("S", bigB[:, par:par + fi], raw[s0])
        for c0 in range(0, fi, 512):
            quant_slice(bigB[:, par:par + fi], bigA[:, par:par + fi],
                        c0, min(512, fi - c0))
        for c in range(fi // 128):
            pb = 5 + (c % 3)
            TR(ps[pb][:, 0:128], bigA[:, par + c * 128:par + (c + 1) * 128], identp[:])
            sslot = bigB[:, 2 * FF + (c % 8) * 128:2 * FF + ((c % 8) + 1) * 128]
            CP(sslot, ps[pb][:, 0:128], eng="S")
            if wsel == "g":
                DMA("P", wq_gu[s0, :, c * 128:(c + 1) * 128], sslot)
            elif wsel == "u":
                DMA("P", wq_gu[FFC + s0, :, c * 128:(c + 1) * 128], sslot)
            else:
                DMA("P", wq_d[s0, :, c * 128:(c + 1) * 128], sslot)

    def load_xtile(src_dram, t, dst=None):
        dst = xtile if dst is None else dst
        DMA("Y", dst[:, 0:EC * NT].rearrange("p (c n) -> p c n", n=NT),
            src_dram[:, :, t * NT:(t + 1) * NT].rearrange("c p n -> p c n"))

    def store_stream(dst_dram, srcbuf, t, q="P"):
        DMA(q, dst_dram[:, :, t * NT:(t + 1) * NT].rearrange("c p n -> p c n"),
            srcbuf[:, 0:EC * NT].rearrange("p (c n) -> p c n", n=NT))

    pre_emitted = [0]

    for s in range(SEQ):
        xsrc = x_in[s]

        MEMSET(kmax2[0:1, 0:16], 0.0)
        load_xtile(xsrc, 0)
        load_xtile(xsrc, 1, dst=sqtile)
        for tp in range(2):
            t0 = 2 * tp
            ln_fmajor(xtile[:, 0:EC * NT], lntile[:, 0:EC * NT])
            ln_fmajor(sqtile[:, 0:EC * NT], qtile[:, 0:EC * NT])
            if tp == 0:
                load_xtile(xsrc, 2)
                load_xtile(xsrc, 3, dst=sqtile)
            for o in range(8, 24):
                for _ in range(2):
                    if s == 0 and pre_emitted[0] < len(pre_jobs):
                        prephase_strip(pre_emitted[0])
                        pre_emitted[0] += 1
                wt = load_piece(w_qkv[o][:, 0:EC * 128], EC * 128)
                for ti in range(2):
                    t = t0 + ti
                    rhs = lntile if ti == 0 else qtile
                    b = pcount[0] % 2
                    pcount[0] += 1
                    for c in range(EC):
                        MM(ps[b][:, 0:NT], wt[:, c * 128:(c + 1) * 128],
                           rhs[:, c * NT:(c + 1) * NT],
                           start=(c == 0), stop=(c == EC - 1))
                    sslot = stage[:, ti * NT:(ti + 1) * NT]
                    TS(sslot, ps[b][:, 0:NT], bcol("qkv", o), Alu.add)
                    if o < 16:
                        DMA("P", k_s[o - 8, :, t * NT:(t + 1) * NT], sslot)
                        ksq = (stage[:, 1280:1280 + NT] if ti == 0
                               else bigB[:, 10752:10752 + NT])
                        ACT(ksq, sslot, Act.Square)
                        for hh in range(2):
                            h = 2 * (o - 8) + hh
                            MM(ps[4][0:1, 0:NT], onesK[hh * 64:(hh + 1) * 64, 0:1],
                               ksq[hh * 64:(hh + 1) * 64, :], start=True, stop=True,
                               tile_position=(hh * 64, 0))
                            RED(small[0:1, 600 + h:601 + h], ps[4][0:1, 0:NT], Alu.max)
                            TT(kmax2[0:1, h:h + 1], kmax2[0:1, h:h + 1],
                               small[0:1, 600 + h:601 + h], Alu.max)
                    else:
                        for sub in range(4):
                            kc = 4 * t + sub
                            for hh in range(2):
                                h = 2 * (o - 16) + hh
                                pbt = 2 + ((sub * 2 + hh) % 3)
                                idsl = identp[hh * 64:(hh + 1) * 64, hh * 64:(hh + 1) * 64]
                                c0 = ti * NT + sub * 128
                                TR(ps[pbt][:, 0:64], stage[hh * 64:(hh + 1) * 64, c0:c0 + 128], idsl)
                                j = (sub * 2 + hh) % 4
                                vslot = stage[:, 1792 + j * 64:1792 + (j + 1) * 64]
                                CP(vslot, ps[pbt][:, 0:64])
                                DMA("P", v_s[kc, :, h, 0:64], vslot)

        while s == 0 and pre_emitted[0] < len(pre_jobs):
            prephase_strip(pre_emitted[0])
            pre_emitted[0] += 1

        for qt in range(TPS):
            load_xtile(xsrc, qt)
            ln_fmajor(xtile[:, 0:EC * NT], lntile[:, 0:EC * NT])
            for o in range(8):
                pb = projp(w_qkv[o], EC, lntile)
                TS(qtile[:, o * NT:(o + 1) * NT], ps[pb][:, 0:NT], bcol("qkv", o), Alu.add)
            for c in range(EC):
                qsq = stage[:, (c % 2) * NT:((c % 2) + 1) * NT]
                ACT(qsq, qtile[:, c * NT:(c + 1) * NT], Act.Square)
                for hh in range(2):
                    h = 2 * c + hh
                    MM(ps[6][0:1, 0:NT], onesK[hh * 64:(hh + 1) * 64, 0:1],
                       qsq[hh * 64:(hh + 1) * 64, :], start=True, stop=True,
                       tile_position=(hh * 64, 0))
                    RED(qmax2[0:1, h:h + 1], ps[6][0:1, 0:NT], Alu.max)
            TT(small2[0:1, 620:636], qmax2[0:1, 0:16], kmax2[0:1, 0:16], Alu.mult)
            ACT(small2[0:1, 640:656], small2[0:1, 620:636], Act.Sqrt)
            TS(small2[0:1, 660:676], small2[0:1, 640:656], -0.125, Alu.mult,
               -80.0, Alu.max)
            MM(ps[6][:, 0:16], onesK[0:1, 0:128], small2[0:1, 660:676], start=True, stop=True)
            CP(negb[:, 0:16], ps[6][:, 0:16])
            MEMSET(vtg.rearrange("p (s w) -> p s w", w=VSL)[:, :, 64:65], 1.0)
            for kcg in range(4):
                kv = ktg.rearrange("p (c n) -> p c n", n=NT)
                DMA("Y", kv[:, 0:4, :],
                    k_s[0:4, :, kcg * NT:(kcg + 1) * NT].rearrange("c p n -> p c n"))
                DMA("Y", kv[:, 4:8, :],
                    k_s[4:8, :, kcg * NT:(kcg + 1) * NT].rearrange("c p n -> p c n"))
                for k4 in range(4):
                    DMA("Y", vtg.rearrange("p (k h w) -> p k h w", h=H, w=VSL)[:, k4, :, 0:64],
                        v_s[kcg * 4 + k4].rearrange("p h x -> p h x"))
                for k4 in range(4):
                    kc = kcg * 4 + k4
                    dsl_i = ditile[:, (k4 % 2) * NT:((k4 % 2) + 1) * NT]
                    IOTA(dsl_i, [[-1, NT]], kc * 128 - qt * NT, 1)
                    ACT(dx4[:, k4 * NT:(k4 + 1) * NT], dsl_i, Act.Abs)
                    ACT(dx4[:, k4 * NT:(k4 + 1) * NT], dx4[:, k4 * NT:(k4 + 1) * NT],
                        Act.Exp, scale=neg_a)
                def ub(hp):
                    return (bigB[:, 0:2048], sqtile[:, 2048:4096],
                            pvu[:, 0:2048])[hp % 3]

                def pb_(hp):
                    return (sqtile[:, 0:2048], lntile[:, 0:2048],
                            pvu[:, 2048:4096])[hp % 3]

                def make_pv(hp, kcg=kcg):
                    hA, hB = 2 * hp, 2 * hp + 1
                    uu, pp = ub(hp), pb_(hp)
                    def run():
                        for k4 in range(4):
                            vA = vtg[:, k4 * H * VSL + hA * VSL: k4 * H * VSL + hA * VSL + 65]
                            MM(ps[2][0:65, 0:NT], vA, uu[:, k4 * NT:(k4 + 1) * NT],
                               start=(k4 == 0), stop=(k4 == 3))
                        for k4 in range(4):
                            vB = vtg[:, k4 * H * VSL + hB * VSL: k4 * H * VSL + hB * VSL + 65]
                            MM(ps[3][0:65, 0:NT], vB, pp[:, k4 * NT:(k4 + 1) * NT],
                               start=(k4 == 0), stop=(k4 == 3))
                        for (hh, bank) in ((hA, 2), (hB, 3)):
                            dstp = pvacc[0:65, hh * NT:(hh + 1) * NT]
                            if kcg == 0:
                                CP(dstp, ps[bank][0:65, 0:NT])
                            else:
                                TT(dstp, ps[bank][0:65, 0:NT], dstp, Alu.add)
                    return run

                pend_pv = []
                for hp in range(8):
                    hA, hB = 2 * hp, 2 * hp + 1
                    cc = hp
                    uu, pp = ub(hp), pb_(hp)
                    for k2 in range(2):
                        for kk in range(2):
                            k4 = k2 * 2 + kk
                            bA = ps[0] if k4 % 2 == 0 else ps[6]
                            bB = ps[1] if k4 % 2 == 0 else ps[7]
                            klA = ktg[0:64, cc * NT + k4 * 128: cc * NT + (k4 + 1) * 128]
                            klB = ktg[64:128, cc * NT + k4 * 128: cc * NT + (k4 + 1) * 128]
                            qrA = qtile[0:64, cc * NT:(cc + 1) * NT]
                            qrB = qtile[64:128, cc * NT:(cc + 1) * NT]
                            MM(bA[:, 0:NT], klA, qrA, start=True, stop=True,
                               tile_position=(0, 0))
                            MM(bB[:, 0:NT], klB, qrB, start=True, stop=True,
                               tile_position=(64, 0))
                            dsl = dx4[:, k4 * NT:(k4 + 1) * NT]
                            STT(uu[:, k4 * NT:(k4 + 1) * NT], bA[:, 0:NT], 0.125,
                                dsl, Alu.mult, Alu.mult)
                            STT(pp[:, k4 * NT:(k4 + 1) * NT], bB[:, 0:NT], 0.125,
                                dsl, Alu.mult, Alu.mult)
                        ACT(uu[:, k2 * 1024:(k2 + 1) * 1024], uu[:, k2 * 1024:(k2 + 1) * 1024],
                            Act.Exp, bias=negb[:, hA:hA + 1])
                        ACT(pp[:, k2 * 1024:(k2 + 1) * 1024], pp[:, k2 * 1024:(k2 + 1) * 1024],
                            Act.Exp, bias=negb[:, hB:hB + 1])
                    if len(pend_pv) == 2:
                        pend_pv.pop(0)()
                    pend_pv.append(make_pv(hp))
                for fn in pend_pv:
                    fn()
            DMA("P", srows[0:16, 0:NT], pvacc[64:65, 0:H * NT])
            RECIP(small2[0:16, 0:NT], srows[0:16, 0:NT])
            DMA("P", bigB[0:1, 0:H * NT], small2[0:16, 0:NT])
            for h in range(H):
                rb = ps[4] if h % 2 == 0 else ps[5]
                MM(rb[:, 0:NT], onesK[0:1, 0:128], bigB[0:1, h * NT:(h + 1) * NT],
                   start=True, stop=True)
                po = (h % 2) * 64
                c = h // 2
                TT(pvu[po:po + 64, c * NT:(c + 1) * NT], pvacc[0:64, h * NT:(h + 1) * NT],
                   rb[0:64, 0:NT], Alu.mult)
            for o in range(EC):
                pb = projp(w_o[o], EC, pvu)
                STT(lntile[:, o * NT:(o + 1) * NT], ps[pb][:, 0:NT], bcol("o", o),
                    xtile[:, o * NT:(o + 1) * NT], Alu.add, Alu.add)
            store_stream(xa_s, lntile, qt)
        if stop_after in ("A", "B"):
            for t in range(TPS):
                load_xtile(xa_s, t, dst=sqtile)
                store_stream(y_out[s], sqtile, t)
            continue

        def ffn(src_dram, dst_dram):
            strip_state["dual"] = True
            def xside_jobs(t):
                jobs = []
                back = []
                jobs.append(lambda t=t: load_xtile(src_dram, t))
                for sub in range(4):
                    for c in range(EC):
                        def j(sub=sub, c=c):
                            pb = 2 + ((sub * EC + c) % 3)
                            TR(ps[pb][:, 0:128],
                               xtile[:, c * NT + sub * 128: c * NT + (sub + 1) * 128],
                               identp[:])
                            CP(pvu[:, sub * E + c * 128:sub * E + (c + 1) * 128],
                               ps[pb][:, 0:128])
                        jobs.append(j)
                    def jln(sub=sub):
                        xv = pvu[:, sub * E:(sub + 1) * E]
                        p.op("V", lambda v, o=bnst[:, 0:6], i=xv[:, 0:512]:
                             v.bn_stats(out=o, in_=i), reads=[xv[:, 0:512]], writes=[bnst[:, 0:6]])
                        p.op("V", lambda v, o=bnst[:, 6:12], i=xv[:, 512:1024]:
                             v.bn_stats(out=o, in_=i), reads=[xv[:, 512:1024]], writes=[bnst[:, 6:12]])
                        p.op("V", lambda v, o=mv[:, 0:2], i=bnst[:, 0:12]:
                             v.bn_aggr(out=o, in_=i), reads=[bnst[:, 0:12]], writes=[mv[:, 0:2]])
                        ACT(mv[:, 2:3], mv[:, 1:2], Act.Sqrt, bias=epsb[:, 0:1])
                        RECIP(mv[:, 3:4], mv[:, 2:3])
                        TS(sqtile[:, sub * E:(sub + 1) * E], xv, mv[:, 0:1], Alu.subtract,
                           mv[:, 3:4], Alu.mult)
                    jobs.append(jln)
                    for half in range(2):
                        jobs.append(lambda sub=sub, half=half: quant_slice(
                            sqtile[:, 0:4 * E], lntile[:, 0:4 * E],
                            sub * E + half * 512, 512))
                    for c in range(EC):
                        def jb(sub=sub, c=c):
                            pb = 2 + ((sub * EC + c) % 3)
                            TR(ps[pb][:, 0:128],
                               lntile[:, sub * E + c * 128:sub * E + (c + 1) * 128],
                               identp[:])
                            CP(qtile[:, c * NT + sub * 128: c * NT + (sub + 1) * 128],
                               ps[pb][:, 0:128])
                        back.append(jb)
                jobs.extend(back)
                return jobs

            def hqf(c, sub):
                return bigA[:, sub * FF + c * 128:sub * FF + (c + 1) * 128]

            def down_rhs(c):
                return bigA[:].rearrange("p (s f) -> p s f", f=FF)[:, :, c * 128:(c + 1) * 128]

            def weave_quant(s, sub, cw):
                def q(s=s, sub=sub, cw=cw):
                    quant_slice(bigA[:, 0:FFC * NT], bigB[:, 0:FFC * NT],
                                sub * FF + s * 512, cw, vabs=False)
                def tr(s=s, sub=sub):
                    for c in range(4 * s, min(4 * s + 4, FFC)):
                        pb = 2 + ((c + sub) % 3)
                        TR(ps[pb][:, 0:128],
                           bigB[:, sub * FF + c * 128:sub * FF + (c + 1) * 128],
                           identp[:])
                        CP(hqf(c, sub), ps[pb][:, 0:128], eng="S")
                return q, tr

            def gup(t):
                fwd_pend = []
                wq_q = []
                wq_t = []
                for o in range(FFC):
                    proj(0, wq_gu[o], EC, qtile, 0)
                    proj(1, wq_gu[FFC + o], EC, qtile, 0)
                    if fwd_pend:
                        fwd_pend.pop(0)()
                    so = (o % 2) * NT
                    ACT(stage[:, so:so + NT], ps[0][:, 0:NT], Act.Silu)
                    TT(stage[:, 1024 + so:1024 + so + NT], stage[:, so:so + NT],
                       ps[1][:, 0:NT], Alu.mult)
                    if wq_t and wq_t[0][0] < o:
                        wq_t.pop(0)[1]()
                    if o >= 4 and (o - 4) % 4 == 0 and (o - 4) // 4 <= 4:
                        s = (o - 4) // 4
                        for sub in range(4):
                            wq_q.append(weave_quant(s, sub, 512))
                    for _ in range(1 if o < 20 else 2):
                        if wq_q:
                            qf, tf = wq_q.pop(0)
                            qf()
                            wq_t.append((o, tf))

                    def fwd(o=o, so=so):
                        for sub in range(4):
                            pb = 2 + ((o + sub) % 3)
                            TR(ps[pb][:, 0:128],
                               stage[:, 1024 + so + sub * 128:1024 + so + (sub + 1) * 128],
                               identp[:])
                            CP(bigA[:, sub * FF + o * 128:sub * FF + (o + 1) * 128],
                               ps[pb][:, 0:128], eng="S")
                    fwd_pend.append(fwd)
                for fn in fwd_pend:
                    fn()
                for sub in range(4):
                    wq_q.append(weave_quant(5, sub, 256))
                ready = [tf for _, tf in wq_t]
                while wq_q:
                    qf, tf = wq_q.pop(0)
                    qf()
                    ready.append(tf)
                for tf in ready:
                    tf()

            def down(t, xjobs):
                nx = len(xjobs)
                for o in range(EC):
                    b = o % 2
                    rslot = stage[:, 1024 + b * NT:1024 + (b + 1) * NT]
                    DMA("Y", rslot, src_dram[o, :, t * NT:(t + 1) * NT])
                    done = 0
                    while done < FFC:
                        take = min(8, FFC - done)
                        wt = load_piece(wq_d[o][:, done * 128:(done + take) * 128], take * 128)
                        for j in range(take):
                            c = done + j
                            MM(ps[b][:, 0:NT], wt[:, j * 128:(j + 1) * 128],
                               down_rhs(c), start=(c == 0), stop=(c == FFC - 1))
                        done += take
                    sslot = stage[:, b * NT:(b + 1) * NT]
                    TT(sslot, ps[b][:, 0:NT], rslot, Alu.add)
                    DMA("P", dst_dram[o, :, t * NT:(t + 1) * NT], sslot)
                    j0 = (o * nx) // EC
                    j1 = ((o + 1) * nx) // EC
                    for j in range(j0, j1):
                        xjobs[j]()

            for j in xside_jobs(0):
                j()
            for t in range(TPS):
                gup(t)
                xjobs = xside_jobs(t + 1) if t + 1 < TPS else []
                down(t, xjobs)
            strip_state["dual"] = False

        ffn(xa_s, xb_s)
        if stop_after == "C":
            for t in range(TPS):
                load_xtile(xb_s, t, dst=sqtile)
                store_stream(y_out[s], sqtile, t)
            continue

        strip_state["dual"] = True
        for t in range(TPS):
            load_xtile(xb_s, t)
            ln_fmajor(xtile[:, 0:EC * NT], lntile[:, 0:EC * NT])
            for o in range(DINC):
                pb = projp4(w_i[o], EC, lntile)
                TS(bigA[:, o * NT:(o + 1) * NT], ps[pb][:, 0:NT], bcol("i", o), Alu.add)
            for o in range(DINC):
                pb = projp4(w_g[o], DINC, bigA)
                so = (o % 4) * NT
                ACT(stage[:, so:so + NT], ps[pb][:, 0:NT], Act.Silu, bias=bcol("g", o))
                TS(bigB[:, o * NT:(o + 1) * NT], stage[:, so:so + NT], bcol("bsp", o), Alu.mult)
            for o in range(EC):
                pb = projp4(w_o2[o], DINC, bigB)
                STT(qtile[:, o * NT:(o + 1) * NT], ps[pb][:, 0:NT], bcol("o2", o),
                    lntile[:, o * NT:(o + 1) * NT], Alu.add, Alu.add)
            ln_fmajor(qtile[:, 0:EC * NT], pvu[:, 0:EC * NT])
            for o in range(EC):
                TS(pvu[:, o * NT:(o + 1) * NT], pvu[:, o * NT:(o + 1) * NT],
                   bcol("nw", o), Alu.mult)
                STT(lntile[:, o * NT:(o + 1) * NT], pvu[:, o * NT:(o + 1) * NT], 1.0,
                    xtile[:, o * NT:(o + 1) * NT], Alu.mult, Alu.add)
                TS(lntile[:, o * NT:(o + 1) * NT], lntile[:, o * NT:(o + 1) * NT],
                   bcol("nb", o), Alu.add)
            store_stream(xa_s, lntile, t)
        strip_state["dual"] = False
        if stop_after == "D":
            for t in range(TPS):
                load_xtile(xa_s, t, dst=sqtile)
                store_stream(y_out[s], sqtile, t)
            continue

        ffn(xa_s, y_out[s])

    p.finalize()
    block = EE(nc.Block())
    p.emit(block, sems)
    ctx.close()
    return nc


def _prep_weight_strips(W_fo_fi):
    fo, fi = W_fo_fi.shape
    Sn, C = fo // 128, fi // 128
    return np.ascontiguousarray(
        W_fo_fi.reshape(Sn, 128, C, 128).transpose(0, 3, 2, 1).reshape(Sn, 128, C * 128)
    ).astype(np.float32)


def _prep_bias(b, n):
    return np.ascontiguousarray(np.asarray(b, np.float32).reshape(n, 128).T)


def kernel(
    L0_attn_a_x, L0_attn_a_Wqkv_w, L0_attn_a_Wqkv_b, L0_attn_a_out_w, L0_attn_a_out_b,
    L0_attn_a_distance_decay, L0_attn_b_W_in, L0_attn_b_b_in, L0_attn_b_basis_real,
    L0_attn_b_basis_imag, L0_attn_b_ln_w, L0_attn_b_ln_b, L0_attn_b_theta, L0_attn_b_omega,
    L0_attn_b_sigma, L0_attn_b_W_sp, L0_attn_b_b_sp, L0_attn_b_W_g, L0_attn_b_b_g,
    L0_attn_b_W_out, L0_attn_b_b_out, L0_attn_b_norm_w, L0_attn_b_norm_b,
    L0_ffn_w_gate, L0_ffn_w_up, L0_ffn_w_down, _stop=None,
):
    f32 = lambda a: np.asarray(a, np.float32)
    x = f32(L0_attn_a_x)

    common = {
        "w_qkv": _prep_weight_strips(f32(L0_attn_a_Wqkv_w)),
        "b_qkv": _prep_bias(L0_attn_a_Wqkv_b, 24),
        "w_o": _prep_weight_strips(f32(L0_attn_a_out_w)),
        "b_o": _prep_bias(L0_attn_a_out_b, EC),
        "w_i": _prep_weight_strips(np.ascontiguousarray(f32(L0_attn_b_W_in).T)),
        "b_i": _prep_bias(L0_attn_b_b_in, DINC),
        "w_g": _prep_weight_strips(np.ascontiguousarray(f32(L0_attn_b_W_g).T)),
        "b_g": _prep_bias(L0_attn_b_b_g, DINC),
        "w_o2": _prep_weight_strips(np.ascontiguousarray(f32(L0_attn_b_W_out).T)),
        "b_o2": _prep_bias(L0_attn_b_b_out, EC),
        "nw_i": _prep_bias(L0_attn_b_norm_w, EC),
        "nb_i": _prep_bias(L0_attn_b_norm_b, EC),
        "bsp": _prep_bias(L0_attn_b_b_sp, DINC),
        "ddv": f32(L0_attn_a_distance_decay).reshape(1, 1),
        "wg_raw": np.ascontiguousarray(f32(L0_ffn_w_gate).reshape(FFC, 128, E)),
        "wu_raw": np.ascontiguousarray(f32(L0_ffn_w_up).reshape(FFC, 128, E)),
        "wd_raw": np.ascontiguousarray(f32(L0_ffn_w_down).reshape(EC, 128, FF)),
    }

    in_maps = []
    for c in range(NCORES):
        m = dict(common)
        m["x_in"] = np.ascontiguousarray(
            x[2 * c:2 * c + 2].transpose(0, 2, 1)).reshape(SEQ, EC, 128, S)
        in_maps.append(m)

    nc = build_nc(stop_after=_stop)
    res = run_bass_kernel_spmd(nc, in_maps, core_ids=list(range(NCORES)))

    out = np.empty((16, S, E), np.float32)
    for c in range(NCORES):
        yc = np.asarray(res.results[c]["y_out"]).reshape(SEQ, E, S)
        out[2 * c:2 * c + 2] = yc.transpose(0, 2, 1)
    return out


# version 28
# speedup vs baseline: 1.0576x; 1.0576x over previous
import contextlib
import numpy as np

import concourse.bass as bass
import concourse.mybir as mybir
from concourse.bass_utils import run_bass_kernel_spmd

F32 = mybir.dt.float32
I32 = mybir.dt.int32
Alu = mybir.AluOpType
Act = mybir.ActivationFunctionType
AxX = mybir.AxisListType.X

NCORES = 8
SEQ = 2
S = 2048
E = 1024
EC = E // 128
H = 16
FF = 2816
FFC = FF // 128
DIN = 2048
DINC = DIN // 128
NT = 512
TPS = S // NT
KC = S // 128

ENGINES = ("T", "V", "S", "P", "Y")
DQ_OF = {"Y": "dY", "P": "dP", "S": "dS"}
NLANES = 10


class Prog:

    def __init__(self):
        self.items = {e: [] for e in ENGINES}
        lanes = [f"{dq}{i}" for dq in DQ_OF.values() for i in range(NLANES)]
        self.lanes = lanes
        self.tick = {k: 0 for k in list(ENGINES) + lanes}
        self.ndma = {dq: 0 for dq in DQ_OF.values()}
        self.know = {e: {} for e in ENGINES}
        self.vc = {k: {} for k in list(ENGINES) + lanes}
        self.writes = {}
        self.reads = {}

    @staticmethod
    def _span(ap):
        key = ap.tensor.name
        lo = int(ap.offset)
        span = 0
        for st, ct in ap.ap:
            span += (int(ct) - 1) * abs(int(st))
        return key, lo, lo + span + 1

    def _collect(self, aps_r, aps_w):
        deps = set()
        for ap in aps_r:
            key, lo, hi = self._span(ap)
            for (l2, h2, cid) in self.writes.get(key, ()):
                if l2 < hi and lo < h2:
                    deps.add(cid)
        for ap in aps_w:
            key, lo, hi = self._span(ap)
            for (l2, h2, cid) in self.writes.get(key, ()):
                if l2 < hi and lo < h2:
                    deps.add(cid)
            for (l2, h2, cid) in self.reads.get(key, ()):
                if l2 < hi and lo < h2:
                    deps.add(cid)
        return deps

    def _record(self, aps_r, aps_w, cid):
        for ap in aps_r:
            key, lo, hi = self._span(ap)
            rl = self.reads.setdefault(key, [])
            rl.append((lo, hi, cid))
            if len(rl) > 256:
                del rl[:128]
        for ap in aps_w:
            key, lo, hi = self._span(ap)
            wl = self.writes.setdefault(key, [])
            wl.append((lo, hi, cid))
            if len(wl) > 192:
                del wl[:96]

    def _emit_wait(self, e, cid):
        pe, t = cid
        if self.know[e].get(pe, 0) >= t:
            return
        self.items[e].append(("w", pe, t))
        self.know[e][pe] = t
        snap = self.vc[pe].get(t)
        if snap:
            k = self.know[e]
            for sk, sv in snap.items():
                if k.get(sk, 0) < sv:
                    k[sk] = sv

    def op(self, e, emit, reads=(), writes=()):
        deps = self._collect(reads, writes)
        for cid in sorted(deps, key=str):
            if e == "T" and cid[0] == "T":
                continue
            self._emit_wait(e, cid)
        self.tick[e] += 1
        t = self.tick[e]
        self.items[e].append(("o", emit))
        snap = dict(self.know[e])
        snap[e] = t
        self.vc[e][t] = snap
        self._record(reads, writes, (e, t))

    def raw(self, e, emit):
        self.items[e].append(("r", emit))

    def dma(self, qe, emit, reads=(), writes=()):
        dq = DQ_OF[qe]
        lane = f"{dq}{self.ndma[dq] % NLANES}"
        self.ndma[dq] += 1
        deps = self._collect(reads, writes)
        if self.tick[lane] > 0:
            deps.add((lane, self.tick[lane]))
        for cid in sorted(deps, key=str):
            self._emit_wait(qe, cid)
        self.tick[lane] += 16
        t = self.tick[lane]
        self.items[qe].append(("d", emit, lane))
        snap = dict(self.know[qe])
        snap[lane] = t
        self.vc[lane][t] = snap
        self._record(reads, writes, (lane, t))

    def finalize(self):
        for qe, dq in DQ_OF.items():
            for i in range(NLANES):
                lane = f"{dq}{i}"
                if self.tick[lane] > 0:
                    self.items[qe].append(("w", lane, self.tick[lane]))

    def emit(self, block, sems):
        waited = {e: set() for e in ENGINES}
        for e in ENGINES:
            for it in self.items[e]:
                if it[0] == "w" and it[1] in waited:
                    waited[it[1]].add(it[2])
        newval = {}
        for e in ENGINES:
            newval[e] = {t: i + 1 for i, t in enumerate(sorted(waited[e]))}

        secmap = {"T": block.tensor, "V": block.vector, "S": block.scalar,
                  "P": block.gpsimd, "Y": block.sync}
        for e in ENGINES:
            items = self.items[e]

            def emit_items(eng, items=items, e=e):
                tick = 0
                for it in items:
                    if it[0] == "w":
                        if it[1] in newval:
                            eng.wait_ge(sems[it[1]], newval[it[1]][it[2]])
                        else:
                            eng.wait_ge(sems[it[1]], it[2])
                    elif it[0] == "o":
                        tick += 1
                        if tick in waited[e]:
                            it[1](eng).then_inc(sems[e], 1)
                        else:
                            it[1](eng)
                    elif it[0] == "r":
                        it[1](eng)
                    else:
                        it[1](eng).then_inc(sems[it[2]], 16)

            secmap[e](emit_items)


def build_nc(stop_after=None):
    nc = bass.Bass("TRN2", target_bir_lowering=False)
    p = Prog()

    dt = lambda name, shape, kind: nc.dram_tensor(name, shape, F32, kind=kind).ap()

    x_in = dt("x_in", [SEQ, EC, 128, S], "ExternalInput")
    w_qkv = dt("w_qkv", [24, 128, EC * 128], "ExternalInput")
    b_qkv = dt("b_qkv", [128, 24], "ExternalInput")
    w_o = dt("w_o", [EC, 128, EC * 128], "ExternalInput")
    b_o = dt("b_o", [128, EC], "ExternalInput")
    w_i = dt("w_i", [DINC, 128, EC * 128], "ExternalInput")
    b_i = dt("b_i", [128, DINC], "ExternalInput")
    w_g = dt("w_g", [DINC, 128, DINC * 128], "ExternalInput")
    b_g = dt("b_g", [128, DINC], "ExternalInput")
    w_o2 = dt("w_o2", [EC, 128, DINC * 128], "ExternalInput")
    b_o2 = dt("b_o2", [128, EC], "ExternalInput")
    nw_i = dt("nw_i", [128, EC], "ExternalInput")
    nb_i = dt("nb_i", [128, EC], "ExternalInput")
    bsp = dt("bsp", [128, DINC], "ExternalInput")
    ddv = dt("ddv", [1, 1], "ExternalInput")
    wg_raw = dt("wg_raw", [FFC, 128, E], "ExternalInput")
    wu_raw = dt("wu_raw", [FFC, 128, E], "ExternalInput")
    wd_raw = dt("wd_raw", [EC, 128, FF], "ExternalInput")
    y_out = dt("y_out", [SEQ, EC, 128, S], "ExternalOutput")

    wq_gu = dt("wq_gu", [2 * FFC, 128, EC * 128], "Internal")
    wq_d = dt("wq_d", [EC, 128, FFC * 128], "Internal")
    k_s = dt("k_s", [EC, 128, S], "Internal")
    v_s = dt("v_s", [KC, 128, H, 64], "Internal")
    xa_s = dt("xa_s", [EC, 128, S], "Internal")
    xb_s = dt("xb_s", [EC, 128, S], "Internal")

    sems = {}
    semkeys = ["T", "V", "S", "P"] + [f"{dq}{i}" for dq in DQ_OF.values()
                                      for i in range(NLANES)]
    for k in semkeys:
        sems[k] = nc.alloc_semaphore(f"sem_{k}")

    ctx = contextlib.ExitStack()
    EE = ctx.enter_context

    def sb(name, cols, dtp=F32):
        return EE(nc.sbuf_tensor(name, [128, cols], dtp))

    xtile = sb("xtile", EC * NT)
    lntile = sb("lntile", EC * NT)
    sqtile = sb("sqtile", EC * NT)
    qtile = sb("qtile", EC * NT)
    pvu = sb("pvu", EC * NT)
    bigA = sb("bigA", FFC * NT)
    bigB = sb("bigB", FFC * NT)
    wtile0 = sb("wtile0", 1024)
    wtile1 = sb("wtile1", 1024)
    stage = sb("stage", 2048)
    accA = sb("accA", NT)
    accB = sb("accB", NT)
    small = sb("small", 1024)
    small2 = sb("small2", 1024)
    srows = sb("srows", NT)
    ones128 = sb("ones128", 128)
    onesK = sb("onesK", 128)
    identp = sb("identp", 128)
    identi = sb("identi", 128, I32)
    ditile = sb("ditile", 2 * NT, I32)
    scl = sb("scl", 256)
    rscl = sb("rscl", 256)
    bias_all = sb("bias_all", 160)
    negb = sb("negb", 16)
    kmax2 = sb("kmax2", 16)
    qmax2 = sb("qmax2", 16)
    bnst = sb("bnst", 16)
    mv = sb("mv", 8)
    dd_sb = sb("dd_sb", 4)
    r6 = sb("r6", 2)
    epsb = sb("epsb", 1)

    ps = [EE(nc.psum_tensor(f"ps{i}", [128, 512], F32)) for i in range(8)]

    pvacc = bigA[:, 0:H * NT]
    dx4 = bigA[:, H * NT:H * NT + 4 * NT]
    ubuf = bigB[:, 0:1024]
    pbuf = bigB[:, 1024:2048]
    ktgo = 2048
    ktg = bigB[:, ktgo:ktgo + EC * NT]
    vtgo = ktgo + EC * NT
    VSL = 65
    vtg = bigB[:, vtgo:vtgo + 4 * H * VSL]

    def TT(out, a, b, op):
        p.op("V", lambda v, out=out, a=a, b=b, op=op:
             v.tensor_tensor(out=out, in0=a, in1=b, op=op),
             reads=[a, b], writes=[out])

    def TS(out, a, s1, op0, s2=None, op1=Alu.bypass):
        rd = [a] + [x for x in (s1, s2) if hasattr(x, "tensor")]
        p.op("V", lambda v, out=out, a=a, s1=s1, s2=s2, op0=op0, op1=op1:
             v.tensor_scalar(out=out, in0=a, scalar1=s1, scalar2=s2, op0=op0, op1=op1),
             reads=rd, writes=[out])

    def STT(out, a, scalar, b, op0, op1):
        rd = [a, b] + ([scalar] if hasattr(scalar, "tensor") else [])
        p.op("V", lambda v, out=out, a=a, sc=scalar, b=b, op0=op0, op1=op1:
             v.scalar_tensor_tensor(out=out, in0=a, scalar=sc, in1=b, op0=op0, op1=op1),
             reads=rd, writes=[out])

    def ACT(out, a, func, bias=None, scale=None):
        kw = {}
        rd = [a]
        if bias is not None:
            kw["bias"] = bias
            if hasattr(bias, "tensor"):
                rd.append(bias)
        if scale is not None:
            kw["scale"] = scale
            if hasattr(scale, "tensor"):
                rd.append(scale)
        p.op("S", lambda s, out=out, a=a, func=func, kw=kw:
             s.activation(out=out, in_=a, func=func, **kw), reads=rd, writes=[out])

    def MM(out, lhsT, rhs, start, stop, tile_position=None):
        p.op("T", lambda t, out=out, l=lhsT, r=rhs, st=start, sp=stop, tp=tile_position:
             t.matmul(out, l, r, start=st, stop=sp, tile_position=tp),
             reads=[lhsT, rhs], writes=[out])

    def TR(out, in_, ident):
        p.op("T", lambda t, out=out, i=in_, d=ident: t.transpose(out, i, d),
             reads=[in_, ident], writes=[out])

    def CP(out, in_, eng="V"):
        if eng == "S":
            p.op("S", lambda s, out=out, i=in_: s.copy(out=out, in_=i),
                 reads=[in_], writes=[out])
        else:
            p.op("V", lambda v, out=out, i=in_: v.tensor_copy(out=out, in_=i),
                 reads=[in_], writes=[out])

    def MEMSET(ap, val, eng="V"):
        p.op(eng, lambda x, ap=ap, val=val: x.memset(ap, val), reads=[], writes=[ap])

    def RED(out, in_, op, absval=None, negate=None):
        p.op("V", lambda v, out=out, i=in_, op=op, av=absval, ng=negate:
             v.tensor_reduce(out=out, in_=i, axis=AxX, op=op,
                             apply_absolute_value=av, negate=ng),
             reads=[in_], writes=[out])

    def RECIP(out, in_):
        p.op("V", lambda v, out=out, i=in_: v.reciprocal(out=out, in_=i),
             reads=[in_], writes=[out])

    def DMA(q, out, in_):
        p.dma(q, lambda g, out=out, in_=in_: g.dma_start(out=out, in_=in_),
              reads=[in_], writes=[out])

    def IOTA(out, pattern, base, cm):
        p.op("P", lambda g, out=out, pat=pattern, b=base, cm=cm:
             g.iota(out, pattern=pat, base=b, channel_multiplier=cm),
             reads=[], writes=[out])

    strip_state = {"buf": 0, "dual": False}

    def load_piece(src_ap, cols):
        b = wtile0 if strip_state["buf"] == 0 else wtile1
        q = "P" if (strip_state["dual"] and strip_state["buf"]) else "Y"
        strip_state["buf"] ^= 1
        DMA(q, b[:, 0:cols], src_ap)
        return b

    def proj(psbank, wsrc, nchunks, rhsbuf, rhs_off):
        done = 0
        while done < nchunks:
            take = min(8, nchunks - done)
            wt = load_piece(wsrc[:, done * 128:(done + take) * 128], take * 128)
            for j in range(take):
                c = done + j
                MM(ps[psbank][:, 0:NT], wt[:, j * 128:(j + 1) * 128],
                   rhsbuf[:, rhs_off + c * NT:rhs_off + (c + 1) * NT],
                   start=(c == 0), stop=(c == nchunks - 1))
            done += take

    pcount = [0]

    def projp(wsrc, nchunks, rhsbuf, rhs_off=0):
        b = pcount[0] % 2
        pcount[0] += 1
        proj(b, wsrc, nchunks, rhsbuf, rhs_off)
        return b

    def projp4(wsrc, nchunks, rhsbuf, banks=(0, 1, 2, 5)):
        b = banks[pcount[0] % len(banks)]
        pcount[0] += 1
        proj(b, wsrc, nchunks, rhsbuf, 0)
        return b

    MEMSET(ones128[:], 1.0 / 1024.0)
    MEMSET(onesK[:], 1.0)
    MEMSET(r6[:, 1:2], 6.0)
    MEMSET(epsb[:, 0:1], 1e-5)
    RECIP(r6[:, 0:1], r6[:, 1:2])
    IOTA(identi[:], [[-1, 128]], 0, 1)
    TS(identp[:], identi[:], 0, Alu.is_equal)
    DMA("Y", dd_sb[0:1, 0:1], ddv[:, :])
    ACT(dd_sb[0:1, 1:2], dd_sb[0:1, 0:1], Act.Abs)
    TS(dd_sb[0:1, 3:4], dd_sb[0:1, 1:2], -1.0, Alu.mult)
    MM(ps[6][:, 0:1], onesK[0:1, 0:128], dd_sb[0:1, 3:4], start=True, stop=True)
    CP(dd_sb[:, 2:3], ps[6][:, 0:1])
    neg_a = dd_sb[:, 2:3]

    BIAS = {}
    off = 0
    for name, src, n in [("qkv", b_qkv, 24), ("o", b_o, EC), ("i", b_i, DINC),
                         ("g", b_g, DINC), ("o2", b_o2, EC), ("nw", nw_i, EC),
                         ("nb", nb_i, EC), ("bsp", bsp, DINC)]:
        BIAS[name] = off
        DMA("Y", bias_all[:, off:off + n], src[:, :])
        off += n

    def bcol(name, j):
        o = BIAS[name] + j
        return bias_all[:, o:o + 1]

    qscnt = [0]

    def quant_slice(src, dst, c0, cw, b5=5, b6=6, b7=7, acc_off=0, vabs=False):
        so = (qscnt[0] % 8) * 32
        qscnt[0] += 1
        bw = cw // 16
        sview = src[:, c0:c0 + cw].rearrange("p (b k) -> p b k", k=16)
        RED(scl[:, so:so + bw], sview, Alu.max, absval=True)
        TS(scl[:, so:so + bw], scl[:, so:so + bw], r6[:, 0:1], Alu.mult, 1e-12, Alu.max)
        RECIP(rscl[:, so:so + bw], scl[:, so:so + bw])
        rb = rscl[:, so:so + bw].unsqueeze(2).broadcast_to([128, bw, 16])
        sclb = scl[:, so:so + bw].unsqueeze(2).broadcast_to([128, bw, 16])
        xn = ps[b5][:, 0:cw]
        TT(xn.rearrange("p (b k) -> p b k", k=16), sview, rb, Alu.mult)
        sgn = ps[b7][:, 0:cw]
        ACT(sgn, xn, Act.Sign)
        axn = ps[b6][:, 0:cw]
        aA = accA[:, acc_off:acc_off + cw]
        aB = accB[:, acc_off:acc_off + cw]
        if vabs:
            TS(aB, xn, -1.0, Alu.mult)
            TT(axn, xn, aB, Alu.max)
        else:
            ACT(axn, xn, Act.Abs)
        TS(aA, axn, 0.25, Alu.is_gt)
        for thr in (0.75, 1.25, 1.75):
            STT(aA, axn, thr, aA, Alu.is_gt, Alu.add)
        TS(aB, axn, 5.0, Alu.is_gt)
        for thr in (5.0, 3.5, 2.5):
            STT(aB, axn, thr, aB, Alu.is_gt, Alu.add)
        G = ps[b5][:, 0:cw]
        STT(G, aA, 0.5, aB, Alu.mult, Alu.add)
        TT(aA.rearrange("p (b k) -> p b k", k=16),
           sgn.rearrange("p (b k) -> p b k", k=16), sclb, Alu.mult)
        TT(dst[:, c0:c0 + cw], G, aA, Alu.mult)

    def quant(src, dst, ncols):
        for c0 in range(0, ncols, 512):
            quant_slice(src, dst, c0, min(512, ncols - c0))

    def ln_fmajor(src, dst):
        for c in range(EC):
            MM(ps[4][:, 0:NT], ones128[:], src[:, c * NT:(c + 1) * NT],
               start=(c == 0), stop=(c == EC - 1))
        for c in range(EC):
            sq = stage[:, (c % 2) * NT:((c % 2) + 1) * NT]
            ACT(sq, src[:, c * NT:(c + 1) * NT], Act.Square)
            MM(ps[3][:, 0:NT], ones128[:], sq, start=(c == 0), stop=(c == EC - 1))
        CP(small[:, 0:NT], ps[4][:, 0:NT])
        TT(small[:, 512:512 + NT], ps[4][:, 0:NT], small[:, 0:NT], Alu.mult)
        TT(small2[:, 0:NT], ps[3][:, 0:NT], small[:, 512:512 + NT], Alu.subtract)
        ACT(small2[:, 512:512 + NT], small2[:, 0:NT], Act.Sqrt, bias=epsb[:, 0:1])
        RECIP(ps[3][:, 0:NT], small2[:, 512:512 + NT])
        for c in range(EC):
            TT(dst[:, c * NT:(c + 1) * NT], src[:, c * NT:(c + 1) * NT],
               ps[4][:, 0:NT], Alu.subtract)
            TT(dst[:, c * NT:(c + 1) * NT], dst[:, c * NT:(c + 1) * NT],
               ps[3][:, 0:NT], Alu.mult)

    pre_jobs = [("g", wg_raw, s0, E) for s0 in range(FFC)] + \
               [("u", wu_raw, s0, E) for s0 in range(FFC)] + \
               [("d", wd_raw, s0, FF) for s0 in range(EC)]

    def prephase_strip(ji):
        wsel, raw, s0, fi = pre_jobs[ji]
        par = (ji % 2) * FF
        DMA("S", bigB[:, par:par + fi], raw[s0])
        for c0 in range(0, fi, 512):
            quant_slice(bigB[:, par:par + fi], bigA[:, par:par + fi],
                        c0, min(512, fi - c0))
        for c in range(fi // 128):
            pb = 5 + (c % 3)
            TR(ps[pb][:, 0:128], bigA[:, par + c * 128:par + (c + 1) * 128], identp[:])
            sslot = bigB[:, 2 * FF + (c % 8) * 128:2 * FF + ((c % 8) + 1) * 128]
            CP(sslot, ps[pb][:, 0:128], eng="S")
            if wsel == "g":
                DMA("P", wq_gu[s0, :, c * 128:(c + 1) * 128], sslot)
            elif wsel == "u":
                DMA("P", wq_gu[FFC + s0, :, c * 128:(c + 1) * 128], sslot)
            else:
                DMA("P", wq_d[s0, :, c * 128:(c + 1) * 128], sslot)

    def load_xtile(src_dram, t, dst=None):
        dst = xtile if dst is None else dst
        DMA("Y", dst[:, 0:EC * NT].rearrange("p (c n) -> p c n", n=NT),
            src_dram[:, :, t * NT:(t + 1) * NT].rearrange("c p n -> p c n"))

    def store_stream(dst_dram, srcbuf, t, q="P"):
        DMA(q, dst_dram[:, :, t * NT:(t + 1) * NT].rearrange("c p n -> p c n"),
            srcbuf[:, 0:EC * NT].rearrange("p (c n) -> p c n", n=NT))

    pre_emitted = [0]

    for s in range(SEQ):
        xsrc = x_in[s]

        MEMSET(kmax2[0:1, 0:16], 0.0)
        load_xtile(xsrc, 0)
        load_xtile(xsrc, 1, dst=sqtile)
        for tp in range(2):
            t0 = 2 * tp
            ln_fmajor(xtile[:, 0:EC * NT], lntile[:, 0:EC * NT])
            ln_fmajor(sqtile[:, 0:EC * NT], qtile[:, 0:EC * NT])
            if tp == 0:
                load_xtile(xsrc, 2)
                load_xtile(xsrc, 3, dst=sqtile)
            for o in range(8, 24):
                for _ in range(2):
                    if s == 0 and pre_emitted[0] < len(pre_jobs):
                        prephase_strip(pre_emitted[0])
                        pre_emitted[0] += 1
                wt = load_piece(w_qkv[o][:, 0:EC * 128], EC * 128)
                for ti in range(2):
                    t = t0 + ti
                    rhs = lntile if ti == 0 else qtile
                    b = pcount[0] % 2
                    pcount[0] += 1
                    for c in range(EC):
                        MM(ps[b][:, 0:NT], wt[:, c * 128:(c + 1) * 128],
                           rhs[:, c * NT:(c + 1) * NT],
                           start=(c == 0), stop=(c == EC - 1))
                    sslot = stage[:, ti * NT:(ti + 1) * NT]
                    TS(sslot, ps[b][:, 0:NT], bcol("qkv", o), Alu.add)
                    if o < 16:
                        DMA("P", k_s[o - 8, :, t * NT:(t + 1) * NT], sslot)
                        ksq = (stage[:, 1280:1280 + NT] if ti == 0
                               else bigB[:, 10752:10752 + NT])
                        ACT(ksq, sslot, Act.Square)
                        for hh in range(2):
                            h = 2 * (o - 8) + hh
                            MM(ps[4][0:1, 0:NT], onesK[hh * 64:(hh + 1) * 64, 0:1],
                               ksq[hh * 64:(hh + 1) * 64, :], start=True, stop=True,
                               tile_position=(hh * 64, 0))
                            RED(small[0:1, 600 + h:601 + h], ps[4][0:1, 0:NT], Alu.max)
                            TT(kmax2[0:1, h:h + 1], kmax2[0:1, h:h + 1],
                               small[0:1, 600 + h:601 + h], Alu.max)
                    else:
                        for sub in range(4):
                            kc = 4 * t + sub
                            for hh in range(2):
                                h = 2 * (o - 16) + hh
                                pbt = 2 + ((sub * 2 + hh) % 3)
                                idsl = identp[hh * 64:(hh + 1) * 64, hh * 64:(hh + 1) * 64]
                                c0 = ti * NT + sub * 128
                                TR(ps[pbt][:, 0:64], stage[hh * 64:(hh + 1) * 64, c0:c0 + 128], idsl)
                                j = (sub * 2 + hh) % 4
                                vslot = stage[:, 1792 + j * 64:1792 + (j + 1) * 64]
                                CP(vslot, ps[pbt][:, 0:64])
                                DMA("P", v_s[kc, :, h, 0:64], vslot)

        while s == 0 and pre_emitted[0] < len(pre_jobs):
            prephase_strip(pre_emitted[0])
            pre_emitted[0] += 1

        for qt in range(TPS):
            load_xtile(xsrc, qt)
            ln_fmajor(xtile[:, 0:EC * NT], lntile[:, 0:EC * NT])
            for o in range(8):
                pb = projp(w_qkv[o], EC, lntile)
                TS(qtile[:, o * NT:(o + 1) * NT], ps[pb][:, 0:NT], bcol("qkv", o), Alu.add)
            for c in range(EC):
                qsq = stage[:, (c % 2) * NT:((c % 2) + 1) * NT]
                ACT(qsq, qtile[:, c * NT:(c + 1) * NT], Act.Square)
                for hh in range(2):
                    h = 2 * c + hh
                    MM(ps[6][0:1, 0:NT], onesK[hh * 64:(hh + 1) * 64, 0:1],
                       qsq[hh * 64:(hh + 1) * 64, :], start=True, stop=True,
                       tile_position=(hh * 64, 0))
                    RED(qmax2[0:1, h:h + 1], ps[6][0:1, 0:NT], Alu.max)
            TT(small2[0:1, 620:636], qmax2[0:1, 0:16], kmax2[0:1, 0:16], Alu.mult)
            ACT(small2[0:1, 640:656], small2[0:1, 620:636], Act.Sqrt)
            TS(small2[0:1, 660:676], small2[0:1, 640:656], -0.125, Alu.mult,
               -80.0, Alu.max)
            MM(ps[6][:, 0:16], onesK[0:1, 0:128], small2[0:1, 660:676], start=True, stop=True)
            CP(negb[:, 0:16], ps[6][:, 0:16])
            MEMSET(vtg.rearrange("p (s w) -> p s w", w=VSL)[:, :, 64:65], 1.0)
            for kcg in range(4):
                kv = ktg.rearrange("p (c n) -> p c n", n=NT)
                DMA("Y", kv[:, 0:4, :],
                    k_s[0:4, :, kcg * NT:(kcg + 1) * NT].rearrange("c p n -> p c n"))
                DMA("Y", kv[:, 4:8, :],
                    k_s[4:8, :, kcg * NT:(kcg + 1) * NT].rearrange("c p n -> p c n"))
                for k4 in range(4):
                    DMA("Y", vtg.rearrange("p (k h w) -> p k h w", h=H, w=VSL)[:, k4, :, 0:64],
                        v_s[kcg * 4 + k4].rearrange("p h x -> p h x"))
                for k4 in range(4):
                    kc = kcg * 4 + k4
                    dsl_i = ditile[:, (k4 % 2) * NT:((k4 % 2) + 1) * NT]
                    IOTA(dsl_i, [[-1, NT]], kc * 128 - qt * NT, 1)
                    ACT(dx4[:, k4 * NT:(k4 + 1) * NT], dsl_i, Act.Abs)
                    ACT(dx4[:, k4 * NT:(k4 + 1) * NT], dx4[:, k4 * NT:(k4 + 1) * NT],
                        Act.Exp, scale=neg_a)
                def ub(hp):
                    return (bigB[:, 0:2048], sqtile[:, 2048:4096],
                            pvu[:, 0:2048])[hp % 3]

                def pb_(hp):
                    return (sqtile[:, 0:2048], lntile[:, 0:2048],
                            pvu[:, 2048:4096])[hp % 3]

                def make_pv(hp, kcg=kcg):
                    hA, hB = 2 * hp, 2 * hp + 1
                    uu, pp = ub(hp), pb_(hp)
                    def run():
                        for k4 in range(4):
                            vA = vtg[:, k4 * H * VSL + hA * VSL: k4 * H * VSL + hA * VSL + 65]
                            MM(ps[2][0:65, 0:NT], vA, uu[:, k4 * NT:(k4 + 1) * NT],
                               start=(k4 == 0), stop=(k4 == 3))
                        for k4 in range(4):
                            vB = vtg[:, k4 * H * VSL + hB * VSL: k4 * H * VSL + hB * VSL + 65]
                            MM(ps[3][0:65, 0:NT], vB, pp[:, k4 * NT:(k4 + 1) * NT],
                               start=(k4 == 0), stop=(k4 == 3))
                        for (hh, bank) in ((hA, 2), (hB, 3)):
                            dstp = pvacc[0:65, hh * NT:(hh + 1) * NT]
                            if kcg == 0:
                                CP(dstp, ps[bank][0:65, 0:NT])
                            else:
                                TT(dstp, ps[bank][0:65, 0:NT], dstp, Alu.add)
                    return run

                pend_pv = []
                for hp in range(8):
                    hA, hB = 2 * hp, 2 * hp + 1
                    cc = hp
                    uu, pp = ub(hp), pb_(hp)
                    for k2 in range(2):
                        for kk in range(2):
                            k4 = k2 * 2 + kk
                            bA = ps[0] if k4 % 2 == 0 else ps[6]
                            bB = ps[1] if k4 % 2 == 0 else ps[7]
                            klA = ktg[0:64, cc * NT + k4 * 128: cc * NT + (k4 + 1) * 128]
                            klB = ktg[64:128, cc * NT + k4 * 128: cc * NT + (k4 + 1) * 128]
                            qrA = qtile[0:64, cc * NT:(cc + 1) * NT]
                            qrB = qtile[64:128, cc * NT:(cc + 1) * NT]
                            MM(bA[:, 0:NT], klA, qrA, start=True, stop=True,
                               tile_position=(0, 0))
                            MM(bB[:, 0:NT], klB, qrB, start=True, stop=True,
                               tile_position=(64, 0))
                            dsl = dx4[:, k4 * NT:(k4 + 1) * NT]
                            STT(uu[:, k4 * NT:(k4 + 1) * NT], bA[:, 0:NT], 0.125,
                                dsl, Alu.mult, Alu.mult)
                            STT(pp[:, k4 * NT:(k4 + 1) * NT], bB[:, 0:NT], 0.125,
                                dsl, Alu.mult, Alu.mult)
                        ACT(uu[:, k2 * 1024:(k2 + 1) * 1024], uu[:, k2 * 1024:(k2 + 1) * 1024],
                            Act.Exp, bias=negb[:, hA:hA + 1])
                        ACT(pp[:, k2 * 1024:(k2 + 1) * 1024], pp[:, k2 * 1024:(k2 + 1) * 1024],
                            Act.Exp, bias=negb[:, hB:hB + 1])
                    if len(pend_pv) == 2:
                        pend_pv.pop(0)()
                    pend_pv.append(make_pv(hp))
                for fn in pend_pv:
                    fn()
            DMA("P", srows[0:16, 0:NT], pvacc[64:65, 0:H * NT])
            RECIP(small2[0:16, 0:NT], srows[0:16, 0:NT])
            DMA("P", bigB[0:1, 0:H * NT], small2[0:16, 0:NT])
            for h in range(H):
                rb = ps[4] if h % 2 == 0 else ps[5]
                MM(rb[:, 0:NT], onesK[0:1, 0:128], bigB[0:1, h * NT:(h + 1) * NT],
                   start=True, stop=True)
                po = (h % 2) * 64
                c = h // 2
                TT(pvu[po:po + 64, c * NT:(c + 1) * NT], pvacc[0:64, h * NT:(h + 1) * NT],
                   rb[0:64, 0:NT], Alu.mult)
            for o in range(EC):
                pb = projp(w_o[o], EC, pvu)
                STT(lntile[:, o * NT:(o + 1) * NT], ps[pb][:, 0:NT], bcol("o", o),
                    xtile[:, o * NT:(o + 1) * NT], Alu.add, Alu.add)
            store_stream(xa_s, lntile, qt)
        if stop_after in ("A", "B"):
            for t in range(TPS):
                load_xtile(xa_s, t, dst=sqtile)
                store_stream(y_out[s], sqtile, t)
            continue

        def ffn(src_dram, dst_dram):
            def xside_jobs(t):
                jobs = []
                back = []
                jobs.append(lambda t=t: load_xtile(src_dram, t))
                for sub in range(4):
                    for c in range(EC):
                        def j(sub=sub, c=c):
                            pb = 2 + ((sub * EC + c) % 3)
                            TR(ps[pb][:, 0:128],
                               xtile[:, c * NT + sub * 128: c * NT + (sub + 1) * 128],
                               identp[:])
                            CP(pvu[:, sub * E + c * 128:sub * E + (c + 1) * 128],
                               ps[pb][:, 0:128])
                        jobs.append(j)
                    def jln(sub=sub):
                        xv = pvu[:, sub * E:(sub + 1) * E]
                        p.op("V", lambda v, o=bnst[:, 0:6], i=xv[:, 0:512]:
                             v.bn_stats(out=o, in_=i), reads=[xv[:, 0:512]], writes=[bnst[:, 0:6]])
                        p.op("V", lambda v, o=bnst[:, 6:12], i=xv[:, 512:1024]:
                             v.bn_stats(out=o, in_=i), reads=[xv[:, 512:1024]], writes=[bnst[:, 6:12]])
                        p.op("V", lambda v, o=mv[:, 0:2], i=bnst[:, 0:12]:
                             v.bn_aggr(out=o, in_=i), reads=[bnst[:, 0:12]], writes=[mv[:, 0:2]])
                        ACT(mv[:, 2:3], mv[:, 1:2], Act.Sqrt, bias=epsb[:, 0:1])
                        RECIP(mv[:, 3:4], mv[:, 2:3])
                        TS(sqtile[:, sub * E:(sub + 1) * E], xv, mv[:, 0:1], Alu.subtract,
                           mv[:, 3:4], Alu.mult)
                    jobs.append(jln)
                    for half in range(2):
                        jobs.append(lambda sub=sub, half=half: quant_slice(
                            sqtile[:, 0:4 * E], lntile[:, 0:4 * E],
                            sub * E + half * 512, 512))
                    for c in range(EC):
                        def jb(sub=sub, c=c):
                            pb = 2 + ((sub * EC + c) % 3)
                            TR(ps[pb][:, 0:128],
                               lntile[:, sub * E + c * 128:sub * E + (c + 1) * 128],
                               identp[:])
                            CP(qtile[:, c * NT + sub * 128: c * NT + (sub + 1) * 128],
                               ps[pb][:, 0:128])
                        back.append(jb)
                jobs.extend(back)
                return jobs

            def hqf(c, sub):
                return bigA[:, sub * FF + c * 128:sub * FF + (c + 1) * 128]

            def down_rhs(c):
                return bigA[:].rearrange("p (s f) -> p s f", f=FF)[:, :, c * 128:(c + 1) * 128]

            def weave_quant(s, sub, cw):
                def q(s=s, sub=sub, cw=cw):
                    quant_slice(bigA[:, 0:FFC * NT], bigB[:, 0:FFC * NT],
                                sub * FF + s * 512, cw, vabs=False)
                def tr(s=s, sub=sub):
                    for c in range(4 * s, min(4 * s + 4, FFC)):
                        pb = 2 + ((c + sub) % 3)
                        TR(ps[pb][:, 0:128],
                           bigB[:, sub * FF + c * 128:sub * FF + (c + 1) * 128],
                           identp[:])
                        CP(hqf(c, sub), ps[pb][:, 0:128], eng="S")
                return q, tr

            def gup(t):
                fwd_pend = []
                wq_q = []
                wq_t = []
                for o in range(FFC):
                    proj(0, wq_gu[o], EC, qtile, 0)
                    proj(1, wq_gu[FFC + o], EC, qtile, 0)
                    if fwd_pend:
                        fwd_pend.pop(0)()
                    so = (o % 2) * NT
                    ACT(stage[:, so:so + NT], ps[0][:, 0:NT], Act.Silu)
                    TT(stage[:, 1024 + so:1024 + so + NT], stage[:, so:so + NT],
                       ps[1][:, 0:NT], Alu.mult)
                    if wq_t and wq_t[0][0] < o:
                        wq_t.pop(0)[1]()
                    if o >= 4 and (o - 4) % 4 == 0 and (o - 4) // 4 <= 4:
                        s = (o - 4) // 4
                        for sub in range(4):
                            wq_q.append(weave_quant(s, sub, 512))
                    for _ in range(1 if o < 20 else 2):
                        if wq_q:
                            qf, tf = wq_q.pop(0)
                            qf()
                            wq_t.append((o, tf))

                    def fwd(o=o, so=so):
                        for sub in range(4):
                            pb = 2 + ((o + sub) % 3)
                            TR(ps[pb][:, 0:128],
                               stage[:, 1024 + so + sub * 128:1024 + so + (sub + 1) * 128],
                               identp[:])
                            CP(bigA[:, sub * FF + o * 128:sub * FF + (o + 1) * 128],
                               ps[pb][:, 0:128], eng="S")
                    fwd_pend.append(fwd)
                for fn in fwd_pend:
                    fn()
                for sub in range(4):
                    wq_q.append(weave_quant(5, sub, 256))
                ready = [tf for _, tf in wq_t]
                while wq_q:
                    qf, tf = wq_q.pop(0)
                    qf()
                    ready.append(tf)
                for tf in ready:
                    tf()

            def down(t, xjobs):
                nx = len(xjobs)
                for o in range(EC):
                    b = o % 2
                    rslot = stage[:, 1024 + b * NT:1024 + (b + 1) * NT]
                    DMA("Y", rslot, src_dram[o, :, t * NT:(t + 1) * NT])
                    done = 0
                    while done < FFC:
                        take = min(8, FFC - done)
                        wt = load_piece(wq_d[o][:, done * 128:(done + take) * 128], take * 128)
                        for j in range(take):
                            c = done + j
                            MM(ps[b][:, 0:NT], wt[:, j * 128:(j + 1) * 128],
                               down_rhs(c), start=(c == 0), stop=(c == FFC - 1))
                        done += take
                    sslot = stage[:, b * NT:(b + 1) * NT]
                    TT(sslot, ps[b][:, 0:NT], rslot, Alu.add)
                    DMA("P", dst_dram[o, :, t * NT:(t + 1) * NT], sslot)
                    j0 = (o * nx) // EC
                    j1 = ((o + 1) * nx) // EC
                    for j in range(j0, j1):
                        xjobs[j]()

            for j in xside_jobs(0):
                j()
            for t in range(TPS):
                gup(t)
                xjobs = xside_jobs(t + 1) if t + 1 < TPS else []
                down(t, xjobs)

        ffn(xa_s, xb_s)
        if stop_after == "C":
            for t in range(TPS):
                load_xtile(xb_s, t, dst=sqtile)
                store_stream(y_out[s], sqtile, t)
            continue

        strip_state["dual"] = True
        for t in range(TPS):
            load_xtile(xb_s, t)
            ln_fmajor(xtile[:, 0:EC * NT], lntile[:, 0:EC * NT])
            for o in range(DINC):
                pb = projp4(w_i[o], EC, lntile)
                TS(bigA[:, o * NT:(o + 1) * NT], ps[pb][:, 0:NT], bcol("i", o), Alu.add)
            for o in range(DINC):
                pb = projp4(w_g[o], DINC, bigA)
                so = (o % 4) * NT
                ACT(stage[:, so:so + NT], ps[pb][:, 0:NT], Act.Silu, bias=bcol("g", o))
                TS(bigB[:, o * NT:(o + 1) * NT], stage[:, so:so + NT], bcol("bsp", o), Alu.mult)
            for o in range(EC):
                pb = projp4(w_o2[o], DINC, bigB)
                STT(qtile[:, o * NT:(o + 1) * NT], ps[pb][:, 0:NT], bcol("o2", o),
                    lntile[:, o * NT:(o + 1) * NT], Alu.add, Alu.add)
            ln_fmajor(qtile[:, 0:EC * NT], pvu[:, 0:EC * NT])
            for o in range(EC):
                TS(pvu[:, o * NT:(o + 1) * NT], pvu[:, o * NT:(o + 1) * NT],
                   bcol("nw", o), Alu.mult)
                STT(lntile[:, o * NT:(o + 1) * NT], pvu[:, o * NT:(o + 1) * NT], 1.0,
                    xtile[:, o * NT:(o + 1) * NT], Alu.mult, Alu.add)
                TS(lntile[:, o * NT:(o + 1) * NT], lntile[:, o * NT:(o + 1) * NT],
                   bcol("nb", o), Alu.add)
            store_stream(xa_s, lntile, t)
        strip_state["dual"] = False
        if stop_after == "D":
            for t in range(TPS):
                load_xtile(xa_s, t, dst=sqtile)
                store_stream(y_out[s], sqtile, t)
            continue

        ffn(xa_s, y_out[s])

    p.finalize()
    block = EE(nc.Block())
    p.emit(block, sems)
    ctx.close()
    return nc


def _prep_weight_strips(W_fo_fi):
    fo, fi = W_fo_fi.shape
    Sn, C = fo // 128, fi // 128
    return np.ascontiguousarray(
        W_fo_fi.reshape(Sn, 128, C, 128).transpose(0, 3, 2, 1).reshape(Sn, 128, C * 128)
    ).astype(np.float32)


def _prep_bias(b, n):
    return np.ascontiguousarray(np.asarray(b, np.float32).reshape(n, 128).T)


def kernel(
    L0_attn_a_x, L0_attn_a_Wqkv_w, L0_attn_a_Wqkv_b, L0_attn_a_out_w, L0_attn_a_out_b,
    L0_attn_a_distance_decay, L0_attn_b_W_in, L0_attn_b_b_in, L0_attn_b_basis_real,
    L0_attn_b_basis_imag, L0_attn_b_ln_w, L0_attn_b_ln_b, L0_attn_b_theta, L0_attn_b_omega,
    L0_attn_b_sigma, L0_attn_b_W_sp, L0_attn_b_b_sp, L0_attn_b_W_g, L0_attn_b_b_g,
    L0_attn_b_W_out, L0_attn_b_b_out, L0_attn_b_norm_w, L0_attn_b_norm_b,
    L0_ffn_w_gate, L0_ffn_w_up, L0_ffn_w_down, _stop=None,
):
    f32 = lambda a: np.asarray(a, np.float32)
    x = f32(L0_attn_a_x)

    common = {
        "w_qkv": _prep_weight_strips(f32(L0_attn_a_Wqkv_w)),
        "b_qkv": _prep_bias(L0_attn_a_Wqkv_b, 24),
        "w_o": _prep_weight_strips(f32(L0_attn_a_out_w)),
        "b_o": _prep_bias(L0_attn_a_out_b, EC),
        "w_i": _prep_weight_strips(np.ascontiguousarray(f32(L0_attn_b_W_in).T)),
        "b_i": _prep_bias(L0_attn_b_b_in, DINC),
        "w_g": _prep_weight_strips(np.ascontiguousarray(f32(L0_attn_b_W_g).T)),
        "b_g": _prep_bias(L0_attn_b_b_g, DINC),
        "w_o2": _prep_weight_strips(np.ascontiguousarray(f32(L0_attn_b_W_out).T)),
        "b_o2": _prep_bias(L0_attn_b_b_out, EC),
        "nw_i": _prep_bias(L0_attn_b_norm_w, EC),
        "nb_i": _prep_bias(L0_attn_b_norm_b, EC),
        "bsp": _prep_bias(L0_attn_b_b_sp, DINC),
        "ddv": f32(L0_attn_a_distance_decay).reshape(1, 1),
        "wg_raw": np.ascontiguousarray(f32(L0_ffn_w_gate).reshape(FFC, 128, E)),
        "wu_raw": np.ascontiguousarray(f32(L0_ffn_w_up).reshape(FFC, 128, E)),
        "wd_raw": np.ascontiguousarray(f32(L0_ffn_w_down).reshape(EC, 128, FF)),
    }

    in_maps = []
    for c in range(NCORES):
        m = dict(common)
        m["x_in"] = np.ascontiguousarray(
            x[2 * c:2 * c + 2].transpose(0, 2, 1)).reshape(SEQ, EC, 128, S)
        in_maps.append(m)

    nc = build_nc(stop_after=_stop)
    res = run_bass_kernel_spmd(nc, in_maps, core_ids=list(range(NCORES)))

    out = np.empty((16, S, E), np.float32)
    for c in range(NCORES):
        yc = np.asarray(res.results[c]["y_out"]).reshape(SEQ, E, S)
        out[2 * c:2 * c + 2] = yc.transpose(0, 2, 1)
    return out
